# version 06
# baseline (speedup 1.0000x reference)
import numpy as np
from contextlib import ExitStack

import concourse.bass as bass
import concourse.mybir as mybir
from concourse.bass_utils import run_bass_kernel_spmd

F32 = mybir.dt.float32
BF16 = mybir.dt.bfloat16
ALU = mybir.AluOpType
AF = mybir.ActivationFunctionType
AX = mybir.AxisListType

B, L, DM = 8, 4096, 1024
TC, NTC = 512, 8
DC = 8
NST, EXP = 16, 16
FF, FC = 4096, 32
TG = 4
EPS = 1e-5
XW = TC + 4

PARANOID = True
NTC_BUILD = NTC
DEBUG_STAGE = ()

SEMS = ("sx", "sw0", "sw1", "sb", "so", "sc", "sp", "cv", "cs", "ct", "cp")
OWN = {"vector": "cv", "scalar": "cs", "gpsimd": "cp"}


class Rec:
    def __init__(self):
        self.items = []
        self.cnt = {s: 0 for s in SEMS}
        self.last_wait = {}

    def emit(self, eng, builder, wait=(), inc=(), fence=None):
        waits = []
        for wv in wait:
            s, v = wv if isinstance(wv, tuple) else (wv, self.cnt[wv])
            if v <= 0:
                continue
            k = (eng, s)
            if self.last_wait.get(k, -1) < v:
                self.last_wait[k] = v
                waits.append((s, v))
        incs = list(inc)
        if fence is None:
            fence = PARANOID and eng in OWN
        if fence and eng in OWN:
            own = OWN[eng]
            v = self.cnt[own]
            k = (eng, own)
            if v > 0 and self.last_wait.get(k, -1) < v:
                self.last_wait[k] = v
                waits.append((own, v))
            if not any(s == own for s, _ in incs):
                incs.append((own, 1))
        for s, d in incs:
            self.cnt[s] += d
        self.items.append((eng, builder, waits, incs))

    def replay(self, name, eng, sems):
        for e, builder, waits, incs in self.items:
            if e != name:
                continue
            for s, v in waits:
                eng.wait_ge(sems[s], v)
            ins = builder(eng)
            for s, d in incs:
                ins = ins.then_inc(sems[s], d)


def build_program():
    nc = bass.Bass("TRN2", target_bir_lowering=False, debug=False)
    dt = nc.dram_tensor

    x_h = dt("x", [L, DM], F32, kind="ExternalInput").ap()
    win_h = dt("win", [DM, 2 * DM], F32, kind="ExternalInput").ap()
    wout_h = dt("wout", [DM, DM], F32, kind="ExternalInput").ap()
    wbr_h = dt("wbr", [128, DC * NST], F32, kind="ExternalInput").ap()
    wcr_h = dt("wcr", [128, DC * NST], F32, kind="ExternalInput").ap()
    binr_h = dt("binr", [128, 16], F32, kind="ExternalInput").ap()
    cwr_h = dt("cwr", [128, DC * 4], F32, kind="ExternalInput").ap()
    cbr_h = dt("cbr", [128, DC], F32, kind="ExternalInput").ap()
    dpr_h = dt("dpr", [128, DC], F32, kind="ExternalInput").ap()
    alr_h = dt("alr", [128, DC * NST], F32, kind="ExternalInput").ap()
    bxb_h = dt("bxb", [16, 1], F32, kind="ExternalInput").ap()
    bxc_h = dt("bxc", [16, 1], F32, kind="ExternalInput").ap()
    bor_h = dt("bor", [1, DM], F32, kind="ExternalInput").ap()
    ktr_h = dt("ktr", [128, DC * NST], F32, kind="ExternalInput").ap()
    rwr_h = dt("rwr", [128, DC * NST], F32, kind="ExternalInput").ap()
    rbr_h = dt("rbr", [16, 1], F32, kind="ExternalInput").ap()
    b1r_h = dt("b1r", [128, EXP * FC], F32, kind="ExternalInput").ap()
    b2_h = dt("b2", [16, DM], F32, kind="ExternalInput").ap()
    w1t_h = dt("w1t", [EXP, DM, FF], F32, kind="ExternalInput").ap()
    w2th_h = dt("w2th", [EXP, 2, FF, 512], F32, kind="ExternalInput").ap()

    y_h = dt("y", [L, DM], F32, kind="ExternalOutput").ap()

    dbg_outs = {}

    def dbg_decl(name, shape, dty):
        if name in DEBUG_STAGE:
            dbg_outs[name] = dt("dbg_" + name, shape, dty,
                                kind="ExternalOutput").ap()

    winb_h = dt("winb", [DM, 2 * DM], BF16, kind="Internal").ap()
    woutb_h = dt("woutb", [DM, DM], BF16, kind="Internal").ap()
    w1b_h = dt("w1b", [EXP, DM, FF], BF16, kind="Internal").ap()
    w2b_h = dt("w2b", [EXP, 2, FF, 512], BF16, kind="Internal").ap()
    bcscr_h = dt("bcscr", [32, TC], F32, kind="Internal").ap()

    for _n, _s, _d in (("fused_e0", [128, 4096], F32),
                       ("ln1", [128, 4096], BF16), ("ln1T", [128, 4096], BF16),
                       ("xext", [128, DC * XW], F32), ("gate", [128, 4096], BF16),
                       ("xconv", [128, 4096], F32), ("BTd", [16, 512], F32),
                       ("CTd", [16, 512], F32), ("yacc", [128, 4096], F32),
                       ("ygated", [128, 4096], BF16), ("x2d", [128, 4096], F32),
                       ("ln2f", [128, 4096], F32), ("ln2Td", [128, 4096], BF16),
                       ("logitsTd", [16, 512], F32), ("lgtd", [128, 64], F32),
                       ("wdensed", [128, 64], F32), ("finit", [128, 4096], F32),
                       ("hidden0", [128, FC * TC], BF16),
                       ("atiled", [128, 128], F32), ("wlogd", [128, 128], F32),
                       ("blogd", [16, 1], F32)):
        dbg_decl(_n, _s, _d)

    with ExitStack() as ctx:
        E = ctx.enter_context
        sbt = nc.sbuf_tensor
        xa0 = E(sbt("xa0", [128, TG * DM], F32))
        xa1 = E(sbt("xa1", [128, TG * DM], F32))
        x2 = E(sbt("x2", [128, TG * DM], F32))
        fused = E(sbt("fused", [128, TG * DM], F32))
        hidden = E(sbt("hidden", [128, FC * TC], BF16))
        ln2T = E(sbt("ln2T", [128, DC * TC], BF16))
        w1s = E(sbt("w1s", [128, 2 * 2048], BF16))
        w2s = E(sbt("w2s", [128, 2 * 1024], BF16))
        xext = E(sbt("xext", [128, DC * XW], F32))
        xconv = E(sbt("xconv", [128, DC * TC], F32))
        gateb = E(sbt("gateb", [128, DC * TC], BF16))
        regD = E(sbt("regD", [128, DC * TC], BF16))
        bb = E(sbt("bb", [128, 2 * TC], F32))
        cbt = E(sbt("cbt", [128, 2 * TC], F32))
        vt = E(sbt("vt", [128, 2 * TC], F32))
        yt = E(sbt("yt", [128, 2 * TC], F32))
        gt = E(sbt("gt", [128, 2 * TC], F32))
        BT = E(sbt("BT", [16, TC], F32))
        CT = E(sbt("CT", [16, TC], F32))
        logitsT = E(sbt("logitsT", [16, TC], F32))
        lgt = E(sbt("lgt", [128, TG * NST], F32))
        wdense = E(sbt("wdense", [128, TG * NST], F32))
        wdT = E(sbt("wdT", [16, TG * 128], F32))
        b2s = E(sbt("b2s", [16, DM], F32))
        b1c = E(sbt("b1c", [128, EXP * FC], F32))
        stats = E(sbt("stats", [128, 16], F32))
        mask_t = E(sbt("mask_t", [128, NST], F32))
        top8 = E(sbt("top8", [128, 8], F32))
        ew = E(sbt("ew", [128, 8], F32))
        wbr = E(sbt("wbrs", [128, DC * NST], BF16))
        wcr = E(sbt("wcrs", [128, DC * NST], BF16))
        binr = E(sbt("binrs", [128, 16], F32))
        cwr = E(sbt("cwrs", [128, DC * 4], F32))
        cbr = E(sbt("cbrs", [128, DC], F32))
        dpr = E(sbt("dprs", [128, DC], F32))
        atile = E(sbt("atile", [128, DC * NST], F32))
        wlog = E(sbt("wlog", [128, DC * NST], F32))
        blog = E(sbt("blog", [16, 1], F32))
        rbr = E(sbt("rbrs", [16, 1], F32))
        bxbs = E(sbt("bxbs", [16, 1], F32))
        bxcs = E(sbt("bxcs", [16, 1], F32))
        borb = E(sbt("borb", [1, DM], BF16))
        onesb = E(sbt("onesb", [1, 128], BF16))
        ones1 = E(sbt("ones1", [128, 1], F32))
        idf = E(sbt("idf", [128, 128], F32))
        idb = E(sbt("idb", [128, 128], BF16))
        H = E(sbt("H", [128, DC * NST], F32))
        tails = E(sbt("tails", [128, DC * 3], F32))
        ph0 = E(nc.psum_tensor("ph0", [128, 512], F32))
        ph1 = E(nc.psum_tensor("ph1", [128, 512], F32))
        po = E(nc.psum_tensor("po", [128, 2048], F32))
        ptb0 = E(nc.psum_tensor("ptb0", [128, 1024], BF16))
        ptb1 = E(nc.psum_tensor("ptb1", [128, 1024], BF16))
        sems = {s: E(nc.semaphore(s)) for s in SEMS}

        xa = (xa0, xa1)
        ph = (ph0, ph1)
        ptb = (ptb0, ptb1)
        hid_f32 = hidden[:].bitcast(F32)
        ln2Tf = hid_f32
        regD_f32 = regD[:].bitcast(F32)

        r = Rec()
        C = r.cnt

        def dump(name, src_ap, j):
            if name not in dbg_outs or j != 0:
                return
            r.emit("sync", lambda e, d=dbg_outs[name], s=src_ap: e.dma_start(
                out=d, in_=s), wait=["cv", "cs", "ct", "cp"], inc=[("so", 16)])
            r.emit("vector", lambda e: e.engine_nop(), wait=[("so", C["so"])])

        def SY(b, **kw):
            r.emit("sync", b, **kw)

        def SC(b, **kw):
            r.emit("scalar", b, **kw)

        def VE(b, **kw):
            r.emit("vector", b, **kw)

        def TE(b, **kw):
            r.emit("tensor", b, **kw)

        def GP(b, **kw):
            r.emit("gpsimd", b, **kw)

        def sw(j):
            return "sw0" if j % 2 == 0 else "sw1"

        ploads = [(binr_h, binr[:]), (cwr_h, cwr[:]), (cbr_h, cbr[:]),
                  (dpr_h, dpr[:]), (alr_h, atile[:]), (ktr_h, wlog[:]),
                  (rwr_h, regD_f32[:, 0:DC * NST]), (rbr_h, rbr[:]),
                  (bxb_h, bxbs[:]), (bxc_h, bxcs[:]), (b1r_h, b1c[:]),
                  (b2_h, b2s[:])]
        for src, dst in ploads:
            SC(lambda e, s=src, d=dst: e.dma_start(out=d, in_=s),
               inc=[("sp", 16)], fence=False)
        SP_ALL = C["sp"]

        GP(lambda e: e.dma_start(out=wbr[:], in_=wbr_h), inc=[("sc", 16)], fence=False)
        GP(lambda e: e.dma_start(out=wcr[:], in_=wcr_h), inc=[("sc", 16)], fence=False)
        GP(lambda e: e.dma_start(out=borb[:], in_=bor_h), inc=[("sc", 16)], fence=False)
        GP(lambda e: e.dma_start(out=winb_h, in_=win_h), inc=[("sc", 16)], fence=False)
        GP(lambda e: e.dma_start(out=woutb_h, in_=wout_h), inc=[("sc", 16)], fence=False)
        SC_PARAMS = C["sc"]
        sc_after_e = []
        for ei in range(EXP):
            GP(lambda e, i=ei: e.dma_start(out=w1b_h[i], in_=w1t_h[i]),
               inc=[("sc", 16)], fence=False)
            GP(lambda e, i=ei: e.dma_start(
                out=w2b_h[i].rearrange("h f d -> (h f) d"),
                in_=w2th_h[i].rearrange("h f d -> (h f) d")),
               inc=[("sc", 16)], fence=False)
            sc_after_e.append(C["sc"])

        SC(lambda e: e.dma_start(
            out=xa0[:].rearrange("p (g d) -> p g d", g=TG),
            in_=x_h[0:TC, :].rearrange("(g p) d -> p g d", p=128)),
           inc=[("sx", 16)], fence=False)

        GP(lambda e: e.memset(idf[:], 0.0))
        GP(lambda e: e.affine_select(out=idf[:], in_=idf[:],
                                     compare_op=ALU.not_equal, fill=1.0, base=0,
                                     pattern=[[-1, 128]], channel_multiplier=1))
        GP(lambda e: e.memset(idb[:], 0.0))
        GP(lambda e: e.affine_select(out=idb[:], in_=idb[:],
                                     compare_op=ALU.not_equal, fill=1.0, base=0,
                                     pattern=[[-1, 128]], channel_multiplier=1))
        GP(lambda e: e.memset(onesb[:], 1.0))
        GP(lambda e: e.memset(ones1[:], 1.0))
        GP_CONST = C["cp"]
        VE(lambda e: e.memset(H[:], 0.0))
        VE(lambda e: e.memset(tails[:], 0.0))

        SC(lambda e: e.activation(out=atile[:], in_=atile[:], func=AF.Exp),
           wait=[("sp", 16 * 5)])
        SC(lambda e: e.activation(out=atile[:], in_=atile[:], func=AF.Exp,
                                  scale=-1.0))
        ATILE_CS = C["cs"]
        SC(lambda e: e.activation(out=regD_f32[:, 256:256 + DC * NST],
                                  in_=wlog[:], func=AF.Square),
           wait=[("sp", 16 * 6)])
        for c in range(DC):
            TE(lambda e, c=c: e.matmul(
                ph0[0:16, 0:1], regD_f32[:, 256 + c * 16:256 + (c + 1) * 16],
                ones1[:], start=(c == 0), stop=(c == DC - 1)),
               wait=(["cs", ("cp", GP_CONST)] if c == 0 else ()),
               inc=([("ct", 1)] if c == DC - 1 else ()))
        VE(lambda e: e.tensor_tensor(out=blog[:], in0=rbr[:], in1=ph0[0:16, 0:1],
                                     op=ALU.subtract),
           wait=["ct", ("sp", SP_ALL)])
        VE(lambda e: e.scalar_tensor_tensor(out=wlog[:], in0=wlog[:], scalar=2.0,
                                            in1=regD_f32[:, 0:DC * NST],
                                            op0=ALU.mult, op1=ALU.add))
        WLOG_CV = C["cv"]
        dump("atiled", atile[:], 0)
        dump("wlogd", wlog[:], 0)
        dump("blogd", blog[:], 0)

        def layernorm(src_ap, dsts, sq_out, extra_wait=()):
            s_sum, s_nm, s_var, s_sd, s_rs = (stats[:, i:i + 1] for i in range(5))
            VE(lambda e: e.reduce_sum(out=s_sum, in_=src_ap, axis=AX.X),
               wait=list(extra_wait))
            VE(lambda e: e.tensor_scalar(out=s_nm, in0=s_sum, scalar1=-1.0 / DM,
                                         scalar2=None, op0=ALU.mult))
            SC(lambda e: e.activation(out=sq_out, in_=src_ap, func=AF.Square,
                                      bias=s_nm, scale=1.0, accum_out=s_var),
               wait=["cv"] + list(extra_wait))
            VE(lambda e: e.tensor_scalar(out=s_var, in0=s_var, scalar1=1.0 / DM,
                                         scalar2=EPS, op0=ALU.mult, op1=ALU.add),
               wait=["cs"])
            SC(lambda e: e.activation(out=s_sd, in_=s_var, func=AF.Sqrt),
               wait=["cv"])
            VE(lambda e: e.reciprocal(out=s_rs, in_=s_sd), wait=["cs"])
            for d_ap in dsts:
                VE(lambda e, d=d_ap: e.tensor_scalar(
                    out=d, in0=src_ap, scalar1=s_nm, scalar2=s_rs,
                    op0=ALU.add, op1=ALU.mult))

        def transpose_bf16(src_f, dst_f, g):
            bank = ptb[g % 2]
            for c in range(DC):
                TE(lambda e, c=c, b=bank: e.transpose(
                    b[:, c * 128:(c + 1) * 128], src_f(c), idb[:]),
                   wait=(["cv", "cs", ("cp", GP_CONST)] if c == 0 else ()),
                   inc=([("ct", 1)] if c == DC - 1 else ()))
            for c in range(DC):
                VE(lambda e, c=c, b=bank: e.tensor_copy(
                    out=dst_f(c), in_=b[:, c * 128:(c + 1) * 128]),
                   wait=["ct"] if c == 0 else ())

        def transpose_f32(src_f, dst_f, g):
            for half in range(2):
                bank = ph[half]
                for s in range(4):
                    c = half * 4 + s
                    TE(lambda e, c=c, s=s, b=bank: e.transpose(
                        b[:, s * 128:(s + 1) * 128], src_f(c), idf[:]),
                       wait=(["cv", "cs", ("cp", GP_CONST)] if s == 0 else ()),
                       inc=([("ct", 1)] if s == 3 else ()))
            for half in range(2):
                bank = ph[half]
                for s in range(4):
                    c = half * 4 + s
                    VE(lambda e, c=c, s=s, b=bank: e.tensor_copy(
                        out=dst_f(c), in_=b[:, s * 128:(s + 1) * 128]),
                       wait=["ct"] if (half == 0 and s == 0) else ())

        ptb0f = ptb0[:].bitcast(F32)
        ptb1f = ptb1[:].bitcast(F32)
        ptbf = (ptb0f, ptb1f)
        ctxs = {}

        def emA1(j):
            xj = xa[j % 2]
            for g in range(TG):
                layernorm(xj[:, g * DM:(g + 1) * DM],
                          [gateb[:, g * DM:(g + 1) * DM]],
                          vt[:, 0:DM],
                          extra_wait=[("sx", 16 * (j + 1))] if g == 0 else ())
            dump("ln1", gateb[:], j)

        def emA2(j):
            for g in range(TG):
                transpose_bf16(
                    lambda c, g=g: gateb[:, g * DM + c * 128: g * DM + (c + 1) * 128],
                    lambda c, g=g: regD[:, c * TC + g * 128: c * TC + (g + 1) * 128],
                    g)
            dump("ln1T", regD[:], j)

        def emA3(j):
            ctx = ctxs[j]
            wint = xa[(j + 1) % 2][:].bitcast(BF16)
            evac_done = {}
            for h in range(2):
                SY(lambda e, h=h: e.dma_start(
                    out=wint.rearrange("p (c m) -> p c m", c=DC),
                    in_=winb_h[:, h * DM:(h + 1) * DM].rearrange(
                        "(c p) m -> p c m", p=128)),
                   wait=["ct", "cv", ("sc", 16 * 4)], inc=[(sw(j), 16)])
                swc = C[sw(j)]
                for ml in range(8):
                    mcg = h * 8 + ml
                    bank = ptbf[mcg % 2]
                    w0 = []
                    if mcg - 2 in evac_done:
                        w0 = [evac_done[mcg - 2]]
                    else:
                        w0 = ["cv", "cs"]
                    for c in range(DC):
                        TE(lambda e, c=c, ml=ml, b=bank, wint=wint: e.matmul(
                            b,
                            wint[:, c * 1024 + ml * 128: c * 1024 + (ml + 1) * 128],
                            regD[:, c * TC:(c + 1) * TC],
                            start=(c == 0), stop=(c == DC - 1)),
                           wait=([(sw(j), swc)] + w0) if c == 0 else (),
                           inc=([("ct", 1)] if c == DC - 1 else ()))
                    if mcg < 8:
                        VE(lambda e, mcg=mcg, b=bank: e.tensor_scalar(
                            out=xext[:, mcg * XW + 4:(mcg + 1) * XW],
                            in0=b, scalar1=binr[:, mcg:mcg + 1], scalar2=None,
                            op0=ALU.add),
                           wait=["ct", ("sp", SP_ALL)])
                        evac_done[mcg] = ("cv", C["cv"])
                    else:
                        SC(lambda e, mcg=mcg, b=bank: e.activation(
                            out=gateb[:, (mcg - 8) * TC:(mcg - 7) * TC],
                            in_=b, func=AF.Silu, bias=binr[:, mcg:mcg + 1],
                            scale=1.0),
                           wait=["ct", ("sp", SP_ALL)])
                        evac_done[mcg] = ("cs", C["cs"])
            ctx["a3_ct"] = C["ct"]
            for c in range(DC):
                VE(lambda e, c=c: e.tensor_copy(
                    out=xext[:, c * XW + 1: c * XW + 4],
                    in_=tails[:, c * 3:(c + 1) * 3]))

        def emA4(j):
            ctx = ctxs[j]
            dump("gate", gateb[:], j)
            for k in (3, 2, 1, 0):
                for c in range(DC):
                    off = c * XW + 1 + k
                    if k == 3:
                        VE(lambda e, c=c, off=off: e.tensor_scalar(
                            out=xconv[:, c * TC:(c + 1) * TC],
                            in0=xext[:, off:off + TC],
                            scalar1=cwr[:, c * 4 + 3:c * 4 + 4], scalar2=None,
                            op0=ALU.mult))
                    else:
                        VE(lambda e, c=c, off=off, k=k: e.scalar_tensor_tensor(
                            out=xconv[:, c * TC:(c + 1) * TC],
                            in0=xext[:, off:off + TC],
                            scalar=cwr[:, c * 4 + k:c * 4 + k + 1],
                            in1=xconv[:, c * TC:(c + 1) * TC],
                            op0=ALU.mult, op1=ALU.add))
            for c in range(DC):
                VE(lambda e, c=c: e.tensor_copy(
                    out=tails[:, c * 3:(c + 1) * 3],
                    in_=xext[:, c * XW + TC + 1: c * XW + TC + 4]))
            for c in range(DC):
                SC(lambda e, c=c: e.activation(
                    out=xconv[:, c * TC:(c + 1) * TC],
                    in_=xconv[:, c * TC:(c + 1) * TC], func=AF.Silu,
                    bias=cbr[:, c:c + 1], scale=1.0), wait=["cv"])
            for c in range(DC):
                SC(lambda e, c=c: e.activation(
                    out=regD[:, c * TC:(c + 1) * TC],
                    in_=xconv[:, c * TC:(c + 1) * TC], func=AF.Copy))
            ctx["xconv_cs"] = C["cs"]
            dump("xext", xext[:], j)
            dump("xconv", xconv[:], j)

        def emA5(j):
            ctx = ctxs[j]
            for (dst, wsrc, bias, bank) in ((BT, wbr, bxbs, ptb0f),
                                            (CT, wcr, bxcs, ptb1f)):
                for c in range(DC):
                    TE(lambda e, c=c, w=wsrc, b=bank: e.matmul(
                        b[0:16, 0:TC], w[:, c * 16:(c + 1) * 16],
                        regD[:, c * TC:(c + 1) * TC],
                        start=(c == 0), stop=(c == DC - 1)),
                       wait=(["cs", "cv", ("sc", 16 * 2)] if c == 0 else ()),
                       inc=([("ct", 1)] if c == DC - 1 else ()))
                VE(lambda e, d=dst, bi=bias, b=bank: e.tensor_scalar(
                    out=d[:], in0=b[0:16, 0:TC], scalar1=bi[0:16, 0:1],
                    scalar2=None, op0=ALU.add), wait=["ct"])
            bcev = C["cv"]
            GP(lambda e: e.dma_start(out=bcscr_h[0:16, :], in_=BT[:]),
               wait=[("cv", bcev)], inc=[("sb", 16)], fence=False)
            GP(lambda e: e.dma_start(out=bcscr_h[16:32, :], in_=CT[:]),
               inc=[("sb", 16)], fence=False)
            ctx["bounce_done"] = C["sb"]
            dump("BTd", BT[:], j)
            dump("CTd", CT[:], j)

        def emD(j):
            ctx = ctxs[j]
            yaccb = xa[(j + 1) % 2]
            for c in range(DC):
                VE(lambda e, c=c, yaccb=yaccb: e.tensor_scalar(
                    out=yaccb[:, c * TC:(c + 1) * TC],
                    in0=xconv[:, c * TC:(c + 1) * TC],
                    scalar1=dpr[:, c:c + 1], scalar2=None, op0=ALU.mult),
                   wait=([("cs", ctx["xconv_cs"]), ("ct", ctx["a3_ct"])]
                         if c == 0 else ()))
            GP(lambda e: e.dma_start(
                out=bb[:, 0:TC], in_=bcscr_h[0:1, :].to_broadcast([128, TC])),
               wait=[("sb", ctx["bounce_done"])], inc=[("sb", 16)], fence=False)
            GP(lambda e: e.dma_start(
                out=cbt[:, 0:TC], in_=bcscr_h[16:17, :].to_broadcast([128, TC])),
               inc=[("sb", 16)], fence=False)
            ctx["bcast"] = {0: C["sb"]}
            ctx["scan_cv"] = {}

        def emScan(j, n_lo, n_hi):
            ctx = ctxs[j]
            yaccb = xa[(j + 1) % 2]
            scan_cv = ctx["scan_cv"]
            for n in range(n_lo, n_hi):
                bcast_done = ctx["bcast"][n]
                if n + 1 < NST:
                    s2 = ((n + 1) % 2) * TC
                    GP(lambda e, n=n, s2=s2: e.dma_start(
                        out=bb[:, s2:s2 + TC],
                        in_=bcscr_h[n + 1:n + 2, :].to_broadcast([128, TC])),
                       inc=[("sb", 16)], fence=False)
                    GP(lambda e, n=n, s2=s2: e.dma_start(
                        out=cbt[:, s2:s2 + TC],
                        in_=bcscr_h[NST + n + 1:NST + n + 2, :].to_broadcast(
                            [128, TC])),
                       inc=[("sb", 16)], fence=False)
                    ctx["bcast"][n + 1] = C["sb"]
                sl = (n % 2) * TC
                for c in range(DC):
                    k = (c % 2) * TC
                    wv = [("sb", bcast_done)] if c == 0 else []
                    if (n, c - 2) in scan_cv:
                        wv.append(("cv", scan_cv[(n, c - 2)]))
                    elif (n - 1, DC - 2 + c) in scan_cv:
                        wv.append(("cv", scan_cv[(n - 1, DC - 2 + c)]))
                    if n == 0 and c == 0:
                        wv += [("cs", ctx["xconv_cs"]), "cv"]
                    GP(lambda e, c=c, k=k, sl=sl: e.tensor_tensor(
                        out=vt[:, k:k + TC],
                        in0=xconv[:, c * TC:(c + 1) * TC],
                        in1=bb[:, sl:sl + TC], op=ALU.mult), wait=wv)
                    VE(lambda e, c=c, n=n, k=k: e.tensor_tensor_scan(
                        out=gt[:, k:k + TC],
                        data0=atile[:, c * 16 + n:c * 16 + n + 1].to_broadcast(
                            [128, TC]),
                        data1=vt[:, k:k + TC],
                        initial=H[:, c * 16 + n:c * 16 + n + 1],
                        op0=ALU.mult, op1=ALU.add),
                       wait=(["cp"] if not (n == 0 and c == 0)
                             else ["cp", ("cs", ATILE_CS)]))
                    scan_cv[(n, c)] = C["cv"]
                    GP(lambda e, k=k, sl=sl: e.tensor_tensor(
                        out=yt[:, k:k + TC], in0=gt[:, k:k + TC],
                        in1=cbt[:, sl:sl + TC], op=ALU.mult), wait=["cv"])
                    ym = C["cp"]
                    GP(lambda e, c=c, n=n, k=k: e.tensor_copy(
                        out=H[:, c * 16 + n:c * 16 + n + 1],
                        in_=gt[:, k + TC - 1: k + TC]))
                    VE(lambda e, c=c, k=k, yaccb=yaccb: e.tensor_tensor(
                        out=yaccb[:, c * TC:(c + 1) * TC],
                        in0=yaccb[:, c * TC:(c + 1) * TC],
                        in1=yt[:, k:k + TC], op=ALU.add),
                       wait=[("cp", ym)])

        def emA8(j):
            yaccb = xa[(j + 1) % 2]
            dump("yacc", yaccb[:], j)
            for c in range(DC):
                VE(lambda e, c=c, yaccb=yaccb: e.tensor_tensor(
                    out=regD[:, c * TC:(c + 1) * TC],
                    in0=yaccb[:, c * TC:(c + 1) * TC],
                    in1=gateb[:, c * TC:(c + 1) * TC], op=ALU.mult),
                   wait=["cs", "cp", "ct"] if c == 0 else ())
            dump("ygated", regD[:], j)

        def emPost(j):
            xj = xa[j % 2]
            SY(lambda e: e.dma_start(
                out=hidden[:, 8192:16384].rearrange("p (c m) -> p c m", c=DC),
                in_=woutb_h.rearrange("(c p) m -> p c m", p=128)),
               wait=["ct", ("sc", 16 * 5)], inc=[(sw(j), 16)])
            swc = C[sw(j)]
            slot_free = {}
            for tg in range(TG):
                for dh in range(2):
                    slot = (tg % 2) * 2 + dh
                    bank = po[:, slot * 512:(slot + 1) * 512]
                    w0 = [(sw(j), swc), "cv"] if (tg == 0 and dh == 0) else []
                    if slot in slot_free:
                        w0.append(("cv", slot_free[slot]))
                    for c in range(DC):
                        TE(lambda e, c=c, tg=tg, dh=dh, b=bank: e.matmul(
                            b, regD[:, c * TC + tg * 128: c * TC + (tg + 1) * 128],
                            hidden[:, 8192 + c * 1024 + dh * 512:
                                   8192 + c * 1024 + (dh + 1) * 512],
                            start=(c == 0), stop=False),
                           wait=w0 if c == 0 else ())
                    TE(lambda e, dh=dh, b=bank: e.matmul(
                        b, onesb[:], borb[:, dh * 512:(dh + 1) * 512],
                        start=False, stop=True),
                       wait=[("sc", 16 * 3), ("cp", GP_CONST)], inc=[("ct", 1)])
                    VE(lambda e, tg=tg, dh=dh, b=bank, xj=xj: e.tensor_tensor(
                        out=x2[:, tg * DM + dh * 512: tg * DM + (dh + 1) * 512],
                        in0=b,
                        in1=xj[:, tg * DM + dh * 512: tg * DM + (dh + 1) * 512],
                        op=ALU.add),
                       wait=["ct"])
                    slot_free[slot] = C["cv"]
            dump("x2d", x2[:], j)
            for g in range(TG):
                layernorm(x2[:, g * DM:(g + 1) * DM],
                          [xext[:, g * DM:(g + 1) * DM],
                           regD[:, g * DM:(g + 1) * DM]],
                          fused[:, 0:DM],
                          extra_wait=[("so", 16 * j)] if g == 0 else ())
            for g in range(TG):
                transpose_bf16(
                    lambda c, g=g: regD[:, g * DM + c * 128: g * DM + (c + 1) * 128],
                    lambda c, g=g: ln2T[:, c * TC + g * 128: c * TC + (g + 1) * 128],
                    g)
            for g in range(TG):
                transpose_f32(
                    lambda c, g=g: xext[:, g * DM + c * 128: g * DM + (c + 1) * 128],
                    lambda c, g=g: ln2Tf[:, c * TC + g * 128: c * TC + (g + 1) * 128],
                    g)
            dump("ln2f", xext[:, 0:4096], j)
            dump("ln2Td", ln2T[:], j)
            for c in range(DC):
                TE(lambda e, c=c: e.matmul(
                    ph0[0:16, 0:TC], wlog[:, c * 16:(c + 1) * 16],
                    ln2Tf[:, c * TC:(c + 1) * TC],
                    start=(c == 0), stop=(c == DC - 1)),
                   wait=(["cv", "cs", ("cv", WLOG_CV)] if c == 0 else ()),
                   inc=([("ct", 1)] if c == DC - 1 else ()))
            VE(lambda e: e.tensor_scalar(out=logitsT[:], in0=ph0[0:16, 0:TC],
                                         scalar1=blog[0:16, 0:1], scalar2=None,
                                         op0=ALU.add),
               wait=["ct"])
            for tg in range(TG):
                TE(lambda e, tg=tg: e.transpose(
                    ph1[:, tg * 16:(tg + 1) * 16],
                    logitsT[:, tg * 128:(tg + 1) * 128], idf[0:16, 0:16]),
                   wait=(["cv", ("cp", GP_CONST)] if tg == 0 else ()),
                   inc=([("ct", 1)] if tg == TG - 1 else ()))
            for tg in range(TG):
                VE(lambda e, tg=tg: e.tensor_copy(
                    out=lgt[:, tg * 16:(tg + 1) * 16],
                    in_=ph1[:, tg * 16:(tg + 1) * 16]),
                   wait=["ct"] if tg == 0 else ())
            dump("logitsTd", logitsT[:], j)
            dump("lgtd", lgt[:], j)
            for tg in range(TG):
                lg_tg = lgt[:, tg * 16:(tg + 1) * 16]
                wd_tg = wdense[:, tg * 16:(tg + 1) * 16]
                VE(lambda e, lg_tg=lg_tg: e.max(out=top8[:], in_=lg_tg))
                VE(lambda e: e.tensor_scalar(out=stats[:, 8:9], in0=top8[:, 0:1],
                                             scalar1=-1.0, scalar2=None,
                                             op0=ALU.mult))
                SC(lambda e: e.activation(out=ew[:, 0:4], in_=top8[:, 0:4],
                                          func=AF.Exp, bias=stats[:, 8:9],
                                          scale=1.0), wait=["cv"])
                VE(lambda e: e.reduce_sum(out=stats[:, 9:10], in_=ew[:, 0:4],
                                          axis=AX.X), wait=["cs"])
                VE(lambda e: e.reciprocal(out=stats[:, 10:11], in_=stats[:, 9:10]))
                VE(lambda e: e.tensor_scalar(out=ew[:, 4:8], in0=ew[:, 0:4],
                                             scalar1=stats[:, 10:11],
                                             scalar2=None, op0=ALU.mult))
                for i in range(4):
                    VE(lambda e, i=i, lg_tg=lg_tg: e.tensor_scalar(
                        out=mask_t[:], in0=lg_tg, scalar1=top8[:, i:i + 1],
                        scalar2=None, op0=ALU.is_equal))
                    if i == 0:
                        VE(lambda e, wd_tg=wd_tg: e.tensor_scalar(
                            out=wd_tg, in0=mask_t[:], scalar1=ew[:, 4:5],
                            scalar2=None, op0=ALU.mult))
                    else:
                        VE(lambda e, i=i, wd_tg=wd_tg: e.scalar_tensor_tensor(
                            out=wd_tg, in0=mask_t[:], scalar=ew[:, 4 + i:5 + i],
                            in1=wd_tg, op0=ALU.mult, op1=ALU.add))
            dump("wdensed", wdense[:], j)
            for tg in range(TG):
                TE(lambda e, tg=tg: e.transpose(
                    ph0[0:16, tg * 128:(tg + 1) * 128],
                    wdense[:, tg * 16:(tg + 1) * 16], idf[:]),
                   wait=(["cv"] if tg == 0 else ()),
                   inc=([("ct", 1)] if tg == TG - 1 else ()))
            for tg in range(TG):
                VE(lambda e, tg=tg: e.tensor_copy(
                    out=wdT[:, tg * 128:(tg + 1) * 128],
                    in_=ph0[0:16, tg * 128:(tg + 1) * 128]),
                   wait=["ct"] if tg == 0 else ())
            a14_free = {}
            for tg in range(TG):
                for dh in range(2):
                    slot = (tg % 2) * 2 + dh
                    bank = po[:, slot * 512:(slot + 1) * 512]
                    wv = ["cv"] if (tg == 0 and dh == 0) else []
                    if slot in a14_free:
                        wv.append(("cv", a14_free[slot]))
                    TE(lambda e, tg=tg, dh=dh, b=bank: e.matmul(
                        b, wdT[0:16, tg * 128:(tg + 1) * 128],
                        b2s[0:16, dh * 512:(dh + 1) * 512],
                        start=True, stop=True),
                       wait=wv, inc=[("ct", 1)])
                    VE(lambda e, tg=tg, dh=dh, b=bank: e.tensor_copy(
                        out=fused[:, tg * DM + dh * 512: tg * DM + (dh + 1) * 512],
                        in_=b),
                       wait=["ct", ("so", 16 * j)])
                    a14_free[slot] = C["cv"]
            ctxs[j]["finit_cv"] = C["cv"]
            dump("finit", fused[:], j)

        def emFend(j):
            for tg in range(TG):
                VE(lambda e, tg=tg: e.tensor_tensor(
                    out=fused[:, tg * DM:(tg + 1) * DM],
                    in0=fused[:, tg * DM:(tg + 1) * DM],
                    in1=x2[:, tg * DM:(tg + 1) * DM], op=ALU.add))
            SC(lambda e, j=j: e.dma_start(
                out=y_h[j * TC:(j + 1) * TC, :].rearrange("(g p) d -> p g d", p=128),
                in_=fused[:].rearrange("p (g d) -> p g d", g=TG)),
               wait=["cv"], inc=[("so", 16)], fence=False)

        def emF(j, inject):
            if j + 1 < NTC_BUILD:
                SC(lambda e, j=j: e.dma_start(
                    out=xa[(j + 1) % 2][:].rearrange("p (g d) -> p g d", g=TG),
                    in_=x_h[(j + 1) * TC:(j + 2) * TC, :].rearrange(
                        "(g p) d -> p g d", p=128)),
                   wait=[("cv", ctxs[j]["finit_cv"])], inc=[("sx", 16)],
                   fence=False)
            pre_w1 = {}
            pre_w2A = {}
            prevB_cmb = None
            prev_hid = None
            for ei in range(EXP):
                for fn in inject.get(ei, ()):
                    fn()
                w1cnt = pre_w1
                w2cntA = pre_w2A
                w2cntB = {}
                pre_w1 = {}
                pre_w2A = {}

                def w1dma(q, eidx, store):
                    SY(lambda e, eidx=eidx, q=q: e.dma_start(
                        out=w1s[:, (q % 2) * 2048:((q % 2) + 1) * 2048].rearrange(
                            "p (c f) -> p c f", c=DC),
                        in_=w1b_h[eidx][:, q * 256:(q + 1) * 256].rearrange(
                            "(c p) f -> p c f", p=128)),
                       wait=[("sc", sc_after_e[eidx]), "ct"], inc=[(sw(j), 16)])
                    store[q] = C[sw(j)]

                def w2dma(h, rr, eidx, store):
                    SY(lambda e, eidx=eidx, h=h, rr=rr: e.dma_start(
                        out=w2s[:, (rr % 2) * 1024:((rr % 2) + 1) * 1024].rearrange(
                            "p (c d) -> p c d", c=2),
                        in_=w2b_h[eidx, h][rr * 256:(rr + 1) * 256, :].rearrange(
                            "(c p) d -> p c d", p=128)),
                       wait=[("sc", sc_after_e[eidx]), "ct"], inc=[(sw(j), 16)])
                    store[rr] = C[sw(j)]

                if 0 not in w1cnt:
                    w1dma(0, ei, w1cnt)
                if 1 not in w1cnt:
                    w1dma(1, ei, w1cnt)
                if 0 not in w2cntA:
                    w2dma(0, 0, ei, w2cntA)
                if 1 not in w2cntA:
                    w2dma(0, 1, ei, w2cntA)
                hid_done = [0] * FC
                for fc in range(FC):
                    q = fc // 2
                    if fc % 2 == 0 and q + 1 < 16 and q + 1 not in w1cnt:
                        w1dma(q + 1, ei, w1cnt)
                    bank = ph[fc % 2]
                    w1base = (q % 2) * 2048 + (fc % 2) * 128
                    w0 = [(sw(j), w1cnt[q])]
                    if fc >= 2:
                        w0.append(("cs", hid_done[fc - 2]))
                    elif prev_hid is None:
                        w0 += ["cs", "cv"]
                    else:
                        w0.append(("cs", prev_hid[30 + fc]))
                    for c in range(DC):
                        TE(lambda e, c=c, b=bank, w1base=w1base: e.matmul(
                            b[:], w1s[:, w1base + c * 256: w1base + c * 256 + 128],
                            ln2T[:, c * TC:(c + 1) * TC],
                            start=(c == 0), stop=(c == DC - 1)),
                           wait=w0 if c == 0 else (),
                           inc=([("ct", 1)] if c == DC - 1 else ()))
                    SC(lambda e, fc=fc, b=bank, ei=ei: e.activation(
                        out=hidden[:, fc * TC:(fc + 1) * TC], in_=b[:],
                        func=AF.Gelu,
                        bias=b1c[:, ei * 32 + fc: ei * 32 + fc + 1], scale=1.0),
                       wait=["ct"])
                    hid_done[fc] = C["cs"]
                    for fcw in ([fc - 1] if fc >= 1 else []) + \
                               ([FC - 1] if fc == FC - 1 else []):
                        rr = fcw // 2
                        w2base = (rr % 2) * 1024 + (fcw % 2) * 512
                        for tg in range(TG):
                            wv = ()
                            if tg == 0:
                                wv = [("cs", hid_done[fcw]), (sw(j), w2cntA[rr])]
                            if fcw == 0:
                                if prevB_cmb is None:
                                    wv = list(wv) + (["cv"] if tg == 0 else [])
                                else:
                                    wv = list(wv) + [("cv", prevB_cmb[tg])]
                            TE(lambda e, fcw=fcw, tg=tg, w2base=w2base: e.matmul(
                                po[:, tg * 512:(tg + 1) * 512],
                                hidden[:, fcw * TC + tg * 128:
                                       fcw * TC + (tg + 1) * 128],
                                w2s[:, w2base:w2base + 512],
                                start=(fcw == 0), stop=(fcw == FC - 1)),
                               wait=wv,
                               inc=([("ct", 1)] if tg == TG - 1 else ()))
                        if fcw % 2 == 1 and fcw // 2 + 2 < 16 \
                                and fcw // 2 + 2 not in w2cntA:
                            w2dma(0, fcw // 2 + 2, ei, w2cntA)
                        if fcw == 29:
                            w2dma(1, 0, ei, w2cntB)
                        if fcw == FC - 1:
                            w2dma(1, 1, ei, w2cntB)
                if ei + 1 < EXP:
                    w1dma(0, ei + 1, pre_w1)
                    w1dma(1, ei + 1, pre_w1)
                cmbA = []
                for tg in range(TG):
                    VE(lambda e, tg=tg, ei=ei: e.scalar_tensor_tensor(
                        out=fused[:, tg * DM: tg * DM + 512],
                        in0=po[:, tg * 512:(tg + 1) * 512],
                        scalar=wdense[:, tg * 16 + ei: tg * 16 + ei + 1],
                        in1=fused[:, tg * DM: tg * DM + 512],
                        op0=ALU.mult, op1=ALU.add),
                       wait=["ct"] if tg == 0 else ())
                    cmbA.append(C["cv"])
                if ei == 0:
                    dump("hidden0", hidden[:], j)
                for fc in range(FC):
                    rr = fc // 2
                    if fc % 2 == 0 and rr + 1 < 16 and rr + 1 not in w2cntB:
                        w2dma(1, rr + 1, ei, w2cntB)
                    w2base = (rr % 2) * 1024 + (fc % 2) * 512
                    for tg in range(TG):
                        wv = ()
                        if tg == 0:
                            wv = [(sw(j), w2cntB[rr])]
                        if fc == 0:
                            wv = list(wv) + [("cv", cmbA[tg])]
                        TE(lambda e, fc=fc, tg=tg, w2base=w2base: e.matmul(
                            po[:, tg * 512:(tg + 1) * 512],
                            hidden[:, fc * TC + tg * 128: fc * TC + (tg + 1) * 128],
                            w2s[:, w2base:w2base + 512],
                            start=(fc == 0), stop=(fc == FC - 1)),
                           wait=wv,
                           inc=([("ct", 1)] if tg == TG - 1 else ()))
                    if fc == 29 and ei + 1 < EXP:
                        w2dma(0, 0, ei + 1, pre_w2A)
                    if fc == FC - 1 and ei + 1 < EXP:
                        w2dma(0, 1, ei + 1, pre_w2A)
                prevB_cmb = []
                for tg in range(TG):
                    VE(lambda e, tg=tg, ei=ei: e.scalar_tensor_tensor(
                        out=fused[:, tg * DM + 512: tg * DM + 1024],
                        in0=po[:, tg * 512:(tg + 1) * 512],
                        scalar=wdense[:, tg * 16 + ei: tg * 16 + ei + 1],
                        in1=fused[:, tg * DM + 512: tg * DM + 1024],
                        op0=ALU.mult, op1=ALU.add),
                       wait=["ct"] if tg == 0 else ())
                    prevB_cmb.append(C["cv"])
                prev_hid = hid_done
                if ei == 0:
                    dump("fused_e0", fused[:], j)

        ctxs[0] = {}
        emA1(0); emA2(0); emA3(0); emA4(0); emA5(0); emD(0)
        emScan(0, 0, NST); emA8(0); emPost(0)
        for j in range(NTC_BUILD):
            inject = {}
            if j + 1 < NTC_BUILD:
                jn = j + 1
                ctxs[jn] = {}
                inject = {
                    0: [lambda jn=jn: emA1(jn)],
                    1: [lambda jn=jn: emA2(jn), lambda jn=jn: emA3(jn)],
                    2: [lambda jn=jn: emA4(jn)],
                    3: [lambda jn=jn: emA5(jn), lambda jn=jn: emD(jn)],
                }
                for k in range(8):
                    inject[4 + k] = [lambda jn=jn, k=k: emScan(jn, 2 * k, 2 * k + 2)]
                inject[12] = [lambda jn=jn: emA8(jn)]
            emF(j, inject)
            emFend(j)
            if j + 1 < NTC_BUILD:
                emPost(j + 1)

        SY(lambda e: e.wait_ge(sems["so"], C["so"]), fence=False)
        SC(lambda e: e.wait_ge(sems["so"], C["so"]), fence=False)

        with nc.Block() as block:
            @block.sync
            def _(eng):
                r.replay("sync", eng, sems)

            @block.scalar
            def _(eng):
                r.replay("scalar", eng, sems)

            @block.vector
            def _(eng):
                r.replay("vector", eng, sems)

            @block.tensor
            def _(eng):
                r.replay("tensor", eng, sems)

            @block.gpsimd
            def _(eng):
                r.replay("gpsimd", eng, sems)

    return nc


def _prep_host(inputs):
    g = lambda k: np.ascontiguousarray(np.asarray(inputs[k], dtype=np.float32))
    W_x = g("L0_attn_W_x")
    keys = g("L0_ffn_keys")
    rw = g("L0_ffn_rw")
    shared = {
        "win": g("L0_attn_W_in"),
        "wout": g("L0_attn_W_out"),
        "wbr": np.ascontiguousarray(
            W_x[:, :16].reshape(8, 128, 16).transpose(1, 0, 2).reshape(128, 128)),
        "wcr": np.ascontiguousarray(
            W_x[:, 16:].reshape(8, 128, 16).transpose(1, 0, 2).reshape(128, 128)),
        "binr": np.ascontiguousarray(g("L0_attn_b_in").reshape(16, 128).T),
        "cwr": np.ascontiguousarray(
            g("L0_attn_conv_w").reshape(8, 128, 4).transpose(1, 0, 2)
            .reshape(128, 32)),
        "cbr": np.ascontiguousarray(g("L0_attn_conv_b").reshape(8, 128).T),
        "dpr": np.ascontiguousarray(g("L0_attn_D_param").reshape(8, 128).T),
        "alr": np.ascontiguousarray(
            g("L0_attn_A_log").reshape(8, 128, 16).transpose(1, 0, 2)
            .reshape(128, 128)),
        "bxb": np.ascontiguousarray(g("L0_attn_b_x")[:16].reshape(16, 1)),
        "bxc": np.ascontiguousarray(g("L0_attn_b_x")[16:].reshape(16, 1)),
        "bor": np.ascontiguousarray(g("L0_attn_b_out").reshape(1, 1024)),
        "ktr": np.ascontiguousarray(
            keys.T.reshape(8, 128, 16).transpose(1, 0, 2).reshape(128, 128)),
        "rwr": np.ascontiguousarray(
            rw.T.reshape(8, 128, 16).transpose(1, 0, 2).reshape(128, 128)),
        "rbr": np.ascontiguousarray(g("L0_ffn_rb").reshape(16, 1)),
        "b1r": np.ascontiguousarray(
            g("L0_ffn_b1").reshape(16, 32, 128).transpose(2, 0, 1)
            .reshape(128, 512)),
        "b2": g("L0_ffn_b2"),
        "w1t": np.ascontiguousarray(g("L0_ffn_w1").transpose(0, 2, 1)),
        "w2th": np.ascontiguousarray(
            g("L0_ffn_w2").transpose(0, 2, 1).reshape(16, 4096, 2, 512)
            .transpose(0, 2, 1, 3)),
    }
    return shared


def kernel(**inputs):
    x = np.ascontiguousarray(np.asarray(inputs["L0_attn_x"], dtype=np.float32))
    shared = _prep_host(inputs)
    nc = build_program()
    in_maps = []
    for b in range(B):
        m = dict(shared)
        m["x"] = np.ascontiguousarray(x[b])
        in_maps.append(m)
    res = run_bass_kernel_spmd(nc, in_maps, core_ids=list(range(B)))
    return np.stack([np.asarray(res.results[b]["y"]) for b in range(B)], axis=0)
